# version 02 (final)
# baseline (speedup 1.0000x reference)
import numpy as np
import ml_dtypes
import concourse.bass as bass
import concourse.mybir as mybir
from concourse.bass_utils import run_bass_kernel_spmd

F32 = mybir.dt.float32
BF16 = mybir.dt.bfloat16
AF = mybir.ActivationFunctionType
ALU = mybir.AluOpType
AXX = mybir.AxisListType.X
BF = ml_dtypes.bfloat16

NCORES = 8
TPC = 1024
C = 128
NCH = TPC // C
H = 16
D = 2048
DC = 16
F = 1024
FC = 8
NE = 16
NSH = 2
EPS = 1e-5

PHASES = 99
DUMP = []
BASE = 16512
XN_O = BASE
QT_O = BASE + 32768
KT_O = BASE + 65536
VT_O = BASE + 98304
OO_O = BASE + 131072
SC_O = BASE + 163840
WS_O = BASE + 180224
SB_O = BASE + 196608
SBB_O = BASE + 204800
TAB_O = BASE + 208896


class Sched:
    ENGS = ("sync", "gpsimd", "tensor", "vector", "scalar")

    def __init__(self):
        self.ops = {e: [] for e in self.ENGS}
        from collections import defaultdict
        self.cnt = defaultdict(int)

    def emit(self, eng, fn, waits=(), inc=None):
        waits = tuple(w for w in waits if w is not None)
        incs = ()
        tok = None
        if inc is not None:
            s, k = inc
            self.cnt[s] += k
            incs = ((s, k),)
            tok = (s, self.cnt[s])
        self.ops[eng].append((waits, fn, incs))
        return tok

    def replay(self, eng_name, engobj, sems):
        last = {}
        for waits, fn, incs in self.ops[eng_name]:
            for s, v in waits:
                if v > last.get(s, 0):
                    engobj.wait_ge(sems[s], v)
                    last[s] = v
            ins = fn(engobj)
            for s, k in incs:
                assert ins is not None
                ins.then_inc(sems[s], k)


def host_tables(core):
    t = {}
    gamma = 1.0 - np.exp2(-5.0 - np.arange(H, dtype=np.float64))
    pos = np.arange(core * TPC, (core + 1) * TPC, dtype=np.float64)
    inv = 1.0 / 10000.0 ** (np.arange(0, 128, 2, dtype=np.float64) / 128.0)
    ang = pos[:, None] * inv[None, :]
    cos = np.cos(ang).T.astype(np.float32)
    sin = np.sin(ang).T.astype(np.float32)
    t["t_cos"] = np.ascontiguousarray(np.concatenate([cos, cos], 0).astype(BF))
    t["t_sin"] = np.ascontiguousarray(np.concatenate([-sin, sin], 0).astype(BF))
    tl = np.arange(C, dtype=np.float64)
    qd = (gamma[:, None] ** (tl[None, :] + 1.0)) * 128.0 ** -0.5
    kd = gamma[:, None] ** (-(tl[None, :] + 1.0))
    t["t_qdec"] = np.ascontiguousarray(
        np.tile(qd.reshape(1, H * C), (128, 1)).astype(BF))
    t["t_kdec"] = np.ascontiguousarray(
        np.tile(kd.reshape(1, H * C), (128, 1)).astype(BF))
    gcb = np.repeat(gamma ** C, 128).reshape(1, H * 128)
    t["t_gcb"] = np.ascontiguousarray(np.tile(gcb, (128, 1)).astype(BF))
    t["t_mask"] = np.ascontiguousarray(
        (tl[None, :] >= tl[:, None]).astype(BF))
    t["t_ident"] = np.eye(128, dtype=BF)
    idE = np.zeros((128, 4, 512), np.float32)
    for il in range(4):
        idE[:, il, il * 128:(il + 1) * 128] = np.eye(128)
    t["t_identE"] = np.ascontiguousarray(idE.reshape(128, 2048).astype(BF))
    t["t_onescol"] = np.ones((128, 1), BF)
    t["t_onesrow"] = np.ones((1, 128), np.float32)
    w = np.zeros((H, NCORES), np.float64)
    for cp in range(core):
        w[:, cp] = (gamma ** TPC) ** (core - 1 - cp)
    wt = np.zeros((1, 128), np.float32)
    wt[0, :] = w.T.reshape(-1).astype(np.float32)
    t["t_wtab"] = np.ascontiguousarray(np.tile(wt, (128, 1)))
    t["t_const"] = np.ascontiguousarray(np.tile(np.array(
        [[EPS, 1.0 / 128.0, 1.0 / D, -1.0 / D]], np.float32), (128, 1)))
    return t


def build_program():
    nc = bass.Bass("TRN2", target_bir_lowering=False, debug=False,
                   detect_race_conditions=False)

    dx = nc.dram_tensor("xs", [TPC, D], F32, kind="ExternalInput").ap()
    dW = {n: nc.dram_tensor(n, [D, D], F32, kind="ExternalInput").ap()
          for n in ("Wq", "Wk", "Wv", "Wg", "Wo")}
    dwa = nc.dram_tensor("wa", [D, NE], F32, kind="ExternalInput").ap()
    dgnw = nc.dram_tensor("gnw", [128, 1], F32, kind="ExternalInput").ap()
    drg = nc.dram_tensor("rg", [NE, D, F], F32, kind="ExternalInput").ap()
    drw1 = nc.dram_tensor("rw1", [NE, D, F], F32, kind="ExternalInput").ap()
    drw2 = nc.dram_tensor("rw2", [NE, F, D], F32, kind="ExternalInput").ap()
    dsg = nc.dram_tensor("sg", [NSH, D, F], F32, kind="ExternalInput").ap()
    dsw1 = nc.dram_tensor("sw1", [NSH, D, F], F32, kind="ExternalInput").ap()
    dsw2 = nc.dram_tensor("sw2", [NSH, F, D], F32, kind="ExternalInput").ap()
    dbias = nc.dram_tensor("biasc", [128, 288], F32, kind="ExternalInput").ap()
    drb2 = nc.dram_tensor("rb2", [NE, D], F32, kind="ExternalInput").ap()
    dsb2 = nc.dram_tensor("sb2", [NSH, D], F32, kind="ExternalInput").ap()
    dtab = {}
    for n, shp, dt in (("t_cos", [128, 1024], BF16), ("t_sin", [128, 1024], BF16),
                       ("t_qdec", [128, 2048], BF16), ("t_kdec", [128, 2048], BF16),
                       ("t_gcb", [128, 2048], BF16), ("t_mask", [128, 128], BF16),
                       ("t_ident", [128, 128], BF16), ("t_identE", [128, 2048], BF16),
                       ("t_onescol", [128, 1], BF16), ("t_onesrow", [1, 128], F32),
                       ("t_wtab", [128, 128], F32), ("t_const", [128, 4], F32)):
        dtab[n] = nc.dram_tensor(n, shp, dt, kind="ExternalInput").ap()
    dout = nc.dram_tensor("out", [TPC, D], F32, kind="ExternalOutput").ap()
    l_local = nc.dram_tensor("l_local", [128, H * 128], F32, kind="Internal")
    l_gather = nc.dram_tensor("l_gather", [NCORES * 128, H * 128], F32,
                              kind="Internal")

    A = nc.alloc_sbuf_tensor_at
    xn = A("xn", [128, DC * TPC], BF16, offset=XN_O)
    qt = A("qt", [128, H * TPC], BF16, offset=QT_O)
    kt = A("kt", [128, H * TPC], BF16, offset=KT_O)
    gate = A("gate", [128, H * TPC], BF16, offset=KT_O)
    vt = A("vt", [128, NCH * D], BF16, offset=VT_O)
    og = A("og", [128, H * TPC], BF16, offset=VT_O)
    oo = A("oo", [128, H * TPC], BF16, offset=OO_O)
    out_acc = A("out_acc", [128, NCH * D], F32, offset=QT_O)
    qdecb = A("qdecb", [128, H * C], BF16, offset=OO_O)
    kdecb = A("kdecb", [128, H * C], BF16, offset=OO_O + 4096)
    wvbuf = [A(f"wv{i}", [128, DC * 256], BF16, offset=OO_O + 8192 + 8192 * i)
             for i in range(2)]
    wvbuf.append(A("wv2", [128, DC * 256], BF16, offset=SC_O))
    ybuf = A("y", [128, FC * TPC], BF16, offset=OO_O)
    w2buf = [A(f"w2_{i}", [128, FC * 1024], BF16, offset=o)
             for i, o in enumerate((VT_O, VT_O + 16384, OO_O + 16384))]
    wobuf = [A(f"wo{i}", [128, DC * 512], BF16, offset=o)
             for i, o in enumerate((WS_O, OO_O))]
    sc_x = A("sc_x", [128, D], F32, offset=SC_O)
    sc_jk = A("sc_jk", [128, D], F32, offset=SC_O + 8192)
    sc_xnt = A("sc_xnt", [128, D], BF16, offset=SC_O + 8192)
    sc_raw = A("sc_raw", [128, 1024], F32, offset=SC_O)
    sc_rot = A("sc_rot", [128, 1024], F32, offset=SC_O + 4096)
    sc_t1 = A("sc_t1", [128, 1024], F32, offset=SC_O + 8192)
    sc_cos = A("sc_cos", [128, 1024], BF16, offset=SC_O + 12288)
    sc_sin = A("sc_sin", [128, 1024], BF16, offset=SC_O + 14336)
    sc_gcb = A("sc_gcb", [128, H * 128], BF16, offset=SC_O)
    sc_am = [A(f"sc_am{i}", [128, 512], BF16, offset=SC_O + 4096 + 1024 * i)
             for i in range(2)]
    sc_st = A("sc_st", [128, 512], F32, offset=SC_O + 8192)
    sc_stage = [A(f"sc_stage{i}", [128, H * 128], F32, offset=SC_O + 8192 * i)
                for i in range(2)]
    sc_osq = A("sc_osq", [128, 1024], BF16, offset=SC_O)
    sc_row = A("sc_row", [1, 1024], F32, offset=SC_O + 2048)
    sc_row2 = A("sc_row2", [1, 1024], F32, offset=SC_O + 6144)
    sc_tmp = A("sc_tmp", [128, 512], F32, offset=SC_O + 10240)
    sc_gel = [A(f"sc_gel{i}", [128, 512], BF16, offset=SC_O + 1024 * i)
              for i in range(2)]
    sc_aff = A("sc_aff", [128, NE], F32, offset=SC_O + 2048)
    sc_am1 = A("sc_am1", [128, NE], F32, offset=SC_O + 2560)
    sc_am2 = A("sc_am2", [128, NE], F32, offset=SC_O + 3072)
    wsbuf = [A(f"ws{i}", [128, DC * 128], BF16, offset=WS_O + 4096 * i)
             for i in range(4)]
    sb_S = A("sb_S", [128, H * 128], F32, offset=SB_O)
    sb_cT = A("sb_cT", [18, NCH * 128], BF16, offset=SB_O)
    sb_idE = A("sb_idE", [128, 2048], BF16, offset=SB_O + 2048)
    sb_wa = A("sb_wa", [128, DC * NE], BF16, offset=SB_O + 6144)
    sbb_S = A("sbb_S", [128, H * 128], BF16, offset=SBB_O)
    sbb_b2 = A("sbb_b2", [18, D], BF16, offset=SBB_O)
    t_mask = A("t_mask", [128, 128], BF16, offset=TAB_O)
    t_ident = A("t_ident", [128, 128], BF16, offset=TAB_O + 256)
    t_onescol = A("t_onescol", [128, 1], BF16, offset=TAB_O + 512)
    t_onesrow = A("t_onesrow", [1, 128], F32, offset=TAB_O + 544)
    t_wtab = A("t_wtab", [128, 128], F32, offset=TAB_O + 1056)
    t_bias = A("t_bias", [128, 288], F32, offset=TAB_O + 1568)
    t_gnw = A("t_gnw", [128, 1], F32, offset=TAB_O + 2720)
    t_comb = A("t_comb", [128, 128], F32, offset=TAB_O + 2752)
    t_combbf = A("t_combbf", [128, 128], BF16, offset=TAB_O + 3264)
    ktr = A("ktr", [128, 128], BF16, offset=TAB_O + 3520)
    t_stat = A("t_stat", [128, 8], F32, offset=TAB_O + 3776)
    t_const = A("t_const", [128, 4], F32, offset=TAB_O + 3808)
    assert TAB_O + 3808 + 16 <= 229344
    nc.const_aps.register(F32, EPS, t_const[:, 0:1])
    nc.const_aps.register(F32, 1.0 / 128.0, t_const[:, 1:2])
    nc.const_aps.register(F32, 1.0 / D, t_const[:, 2:3])
    nc.const_aps.register(F32, -1.0 / D, t_const[:, 3:4])

    P = [nc.alloc_psum_tensor(f"P{i}", [128, 512], F32) for i in range(6)]
    PT = [nc.alloc_psum_tensor(f"PT{i}", [128, 1024], BF16) for i in range(2)]

    S = Sched()
    BG = {i: None for i in list(range(6)) + ["T0", "T1"]}

    def dma(eng, out, in_, waits=(), sem=None, accum=None):
        def fn(e, out=out, in_=in_, accum=accum):
            if accum is not None:
                return e.dma_start(out=out, in_=in_, accum_op=accum)
            return e.dma_start(out=out, in_=in_)
        return S.emit(eng, fn, waits=waits, inc=(sem, 16))

    dumpable = {}
    xr = dx.rearrange("(a p) d -> p a d", p=128)
    xnv = xn[:, :].rearrange("p (c t) -> p c t", c=DC)
    qtv = qt[:, :].rearrange("p (h t) -> p h t", h=H)
    ktv = kt[:, :].rearrange("p (h t) -> p h t", h=H)
    gatev = gate[:, :].rearrange("p (h t) -> p h t", h=H)
    vtv = vt[:, :].rearrange("p (a e) -> p a e", a=NCH)
    oov = oo[:, :].rearrange("p (h t) -> p h t", h=H)
    ogv = og[:, :].rearrange("p (h t) -> p h t", h=H)
    oav = out_acc[:, :].rearrange("p (a d) -> p a d", a=NCH)
    sbSv = sb_S[:, :].rearrange("p (h e) -> p h e", h=H)
    sbBv = sbb_S[:, :].rearrange("p (h e) -> p h e", h=H)
    gcbv = sc_gcb[:, :].rearrange("p (h e) -> p h e", h=H)
    combv = t_comb[:, :].rearrange("p (a eN) -> p a eN", a=NCH)

    tl = []
    for sb, dr in ((qdecb, dtab["t_qdec"]), (kdecb, dtab["t_kdec"]),
                   (t_mask, dtab["t_mask"]), (t_ident, dtab["t_ident"]),
                   (t_onescol, dtab["t_onescol"]), (t_onesrow, dtab["t_onesrow"]),
                   (t_wtab, dtab["t_wtab"]), (t_bias, dbias), (t_gnw, dgnw),
                   (t_const, dtab["t_const"])):
        tl.append(dma("sync", sb[:, :], dr, sem="tb"))
    tok_tab = ("tb", S.cnt["tb"])
    tok_ident = tok_tab
    tok_dec = tok_tab

    prev_xuser = None
    tok_xn_parts = []
    for tt in range(NCH):
        w_x = dma("sync", sc_x[:, :], xr[:, tt], waits=[prev_xuser], sem="xl")
        t_sq = S.emit("scalar", lambda e: e.activation(
            out=sc_jk[:, :], in_=sc_x[:, :], func=AF.Square,
            accum_out=t_stat[:, 0:1]), waits=[w_x], inc=("se", 1))
        t_sum = S.emit("vector", lambda e: e.reduce_sum(
            out=t_stat[:, 1:2], in_=sc_x[:, :], axis=AXX), waits=[w_x],
            inc=("ve", 1))
        t_nm = S.emit("scalar", lambda e: e.mul(t_stat[:, 3:4], t_stat[:, 1:2],
                                                -1.0 / D),
                      waits=[t_sum], inc=("se", 1))
        t_m = S.emit("scalar", lambda e: e.mul(t_stat[:, 2:3], t_stat[:, 1:2],
                                               1.0 / D),
                     waits=[t_sum], inc=("se", 1))
        t_v1 = S.emit("scalar", lambda e: e.mul(t_stat[:, 4:5], t_stat[:, 0:1],
                                                1.0 / D),
                      waits=[t_sq], inc=("se", 1))
        t_m2 = S.emit("vector", lambda e: e.tensor_mul(
            t_stat[:, 5:6], t_stat[:, 2:3], t_stat[:, 2:3]), waits=[t_m],
            inc=("ve", 1))
        t_var = S.emit("vector", lambda e: e.tensor_sub(
            t_stat[:, 6:7], t_stat[:, 4:5], t_stat[:, 5:6]),
            waits=[t_v1, t_m2], inc=("ve", 1))
        t_sd = S.emit("scalar", lambda e: e.activation(
            out=t_stat[:, 7:8], in_=t_stat[:, 6:7], func=AF.Sqrt, bias=EPS),
            waits=[t_var], inc=("se", 1))
        t_rs = S.emit("vector", lambda e: e.reciprocal(
            t_stat[:, 6:7], t_stat[:, 7:8]), waits=[t_sd], inc=("ve", 1))
        t_xnt = S.emit("vector", lambda e: e.tensor_scalar(
            out=sc_xnt[:, :], in0=sc_x[:, :], scalar1=t_stat[:, 3:4],
            scalar2=t_stat[:, 6:7], op0=ALU.add, op1=ALU.mult),
            waits=[t_rs, t_nm, t_sq], inc=("ve", 1))
        prev_xuser = t_xnt
        xnt_v = sc_xnt[:, :].rearrange("p (c u) -> p c u", c=DC)
        last = None
        for dc in range(DC):
            key = "T%d" % (dc % 2)
            slot = PT[dc % 2][:, (dc // 2 % 4) * 128:(dc // 2 % 4) * 128 + 128]
            t_tr = S.emit("tensor", lambda e, slot=slot, xv=xnt_v[:, dc]: e.transpose(
                out=slot, in_=xv, identity=t_ident[:, :]),
                waits=[t_xnt, tok_ident, BG[key]], inc=("te", 1))
            last = S.emit("scalar", lambda e, slot=slot, dc=dc, tt=tt: e.copy(
                out=xnv[:, dc, tt * 128:tt * 128 + 128], in_=slot),
                waits=[t_tr], inc=("se", 1))
            BG[key] = last
        tok_xn_parts.append(last)
    tok_xn = tok_xn_parts[-1]
    dumpable["xn"] = (xn, [tok_xn])
    dma("sync", sc_cos[:, :], dtab["t_cos"], waits=[prev_xuser], sem="cs")
    tok_cossin = dma("sync", sc_sin[:, :], dtab["t_sin"], waits=[prev_xuser], sem="cs")

    wsv = [b[:, :].rearrange("p (c f) -> p c f", c=DC) for b in wsbuf]

    def fm_proj(dst_v, dram_w, dectab, do_rope, evac_silu, ws_guard,
                rope_guard):
        wr = dram_w.rearrange("(c p) f -> p c f", p=128)
        ws_free = list(ws_guard)
        rope_prev = rope_guard
        tok_done = None
        for of in range(H):
            wdma = dma("gpsimd", wsbuf[of % 4][:, :],
                       wr[:, :, of * 128:(of + 1) * 128],
                       waits=[ws_free[of % 4]], sem="w%d" % (of % 4))
            mm_toks = []
            for t2 in range(2):
                bank = (of % 2) * 2 + t2
                def mmg(e, of=of, t2=t2, bank=bank):
                    ins = None
                    for dc in range(DC):
                        ins = e.matmul(out=P[bank][:, :], lhsT=wsv[of % 4][:, dc],
                                       rhs=xnv[:, dc, t2 * 512:t2 * 512 + 512],
                                       start=(dc == 0), stop=(dc == DC - 1))
                    return ins
                mm_toks.append(S.emit("tensor", mmg,
                                      waits=[wdma, tok_xn, BG[bank]],
                                      inc=("te", 1)))
            ws_free[of % 4] = mm_toks[-1]
            if do_rope:
                evs = []
                for t2 in range(2):
                    bank = (of % 2) * 2 + t2
                    ev = S.emit("scalar", lambda e, t2=t2, bank=bank: e.copy(
                        out=sc_raw[:, t2 * 512:t2 * 512 + 512], in_=P[bank][:, :]),
                        waits=[mm_toks[t2], rope_prev], inc=("se", 1))
                    BG[bank] = ev
                    evs.append(ev)
                r1 = dma("sync", sc_rot[0:64, :], sc_raw[64:128, :],
                         waits=[evs[1]], sem="rt")
                r2 = dma("sync", sc_rot[64:128, :], sc_raw[0:64, :],
                         waits=[evs[1]], sem="rt")
                m1 = S.emit("vector", lambda e: e.tensor_mul(
                    sc_t1[:, :], sc_raw[:, :], sc_cos[:, :]),
                    waits=[evs[1], tok_cossin], inc=("ve", 1))
                m2 = S.emit("vector", lambda e: e.tensor_mul(
                    sc_raw[:, :], sc_rot[:, :], sc_sin[:, :]),
                    waits=[r1, r2, m1], inc=("ve", 1))
                s3 = S.emit("vector", lambda e: e.tensor_add(
                    sc_t1[:, :], sc_t1[:, :], sc_raw[:, :]), waits=[m2],
                    inc=("ve", 1))
                dv = dectab[:, :].rearrange("p (h u) -> p h u", h=H)[
                    :, of, None, :].broadcast_to((128, NCH, 128))
                tok_done = S.emit("vector", lambda e, of=of, dv=dv: e.tensor_tensor(
                    out=dst_v[:, of].rearrange("p (a u) -> p a u", a=NCH),
                    in0=sc_t1[:, :].rearrange("p (a u) -> p a u", a=NCH),
                    in1=dv, op=ALU.mult), waits=[s3, tok_dec], inc=("ve", 1))
                rope_prev = tok_done
            else:
                for t2 in range(2):
                    bank = (of % 2) * 2 + t2
                    if evac_silu:
                        fn = (lambda e, of=of, t2=t2, bank=bank: e.activation(
                            out=dst_v[:, of, t2 * 512:t2 * 512 + 512],
                            in_=P[bank][:, :], func=AF.Silu))
                    else:
                        fn = (lambda e, of=of, t2=t2, bank=bank: e.copy(
                            out=dst_v[:, of, t2 * 512:t2 * 512 + 512],
                            in_=P[bank][:, :]))
                    tok_done = S.emit("scalar", fn, waits=[mm_toks[t2]],
                                      inc=("se", 1))
                    BG[bank] = tok_done
        return tok_done, ws_free, rope_prev

    tok_q, ws_free, rope_q = fm_proj(qtv, dW["Wq"], qdecb, True, False,
                                     [tok_xn] * 4, tok_tab)
    tok_k, ws_free, rope_k = fm_proj(ktv, dW["Wk"], kdecb, True, False,
                                     ws_free, rope_q)

    wvr = dW["Wv"].rearrange("(c p) f -> p c f", p=128)
    wvv = [b[:, :].rearrange("p (c f) -> p c f", c=DC) for b in wvbuf]
    wv_free = [tok_dec, None, rope_k]
    wv_free = [None, None, rope_k]
    tok_v = None
    for et in range(8):
        wdma = dma("gpsimd", wvbuf[et % 3][:, :],
                   wvr[:, :, et * 256:(et + 1) * 256],
                   waits=[wv_free[et % 3]], sem="v%d" % (et % 3))
        for tt in range(NCH):
            bank = 4 + tt % 2
            def vmm(e, et=et, tt=tt, bank=bank):
                ins = None
                for dc in range(DC):
                    ins = e.matmul(out=P[bank][:, 0:256],
                                   lhsT=xnv[:, dc, tt * 128:tt * 128 + 128],
                                   rhs=wvv[et % 3][:, dc],
                                   start=(dc == 0), stop=(dc == DC - 1))
                return ins
            mm = S.emit("tensor", vmm, waits=[wdma, tok_xn, BG[bank]],
                        inc=("te", 1))
            if tt == NCH - 1:
                wv_free[et % 3] = mm
            ev = S.emit("scalar", lambda e, et=et, tt=tt, bank=bank: e.copy(
                out=vtv[:, tt, et * 256:(et + 1) * 256], in_=P[bank][:, 0:256]),
                waits=[mm], inc=("se", 1))
            BG[bank] = ev
            tok_v = ev

    dumpable["qt"] = (qt, [tok_q])
    dumpable["kt"] = (kt, [tok_k])
    dumpable["vt"] = (vt, [tok_v])
    final_deps = [tok_q, tok_k, tok_v, tok_xn]
    if PHASES < 3:
        tok_store = dma("sync", dout.rearrange("(a p) d -> p a d", p=128),
                        out_acc[:, :].rearrange("p (a d) -> p a d", a=NCH),
                        waits=final_deps, sem="st")
        S.emit("sync", lambda e: e.nop(), waits=[tok_store])
        return _finish(nc, S, dumpable)
    tok_gcb = dma("sync", sc_gcb[:, :], dtab["t_gcb"], waits=[tok_v], sem="gb")
    tok_S0 = S.emit("vector", lambda e: e.memset(sb_S[:, :], 0.0),
                    waits=[], inc=("ve", 1))
    tok_state = {g: tok_S0 for g in range(4)}
    tok_sbb = {g: tok_S0 for g in range(4)}
    am_free = {0: None, 1: None}
    last_ktr_user = None
    last_oev = {}
    for n in range(NCH):
        for g in range(4):
            hs = [4 * g + i for i in range(4)]
            ba, bo, bkv = g % 2, 2 + g % 2, 4 + g % 2
            amm = []
            kv_toks = []
            for i, h in enumerate(hs):
                tkey = "T%d" % (i % 2)
                slot = PT[i % 2][:, (i // 2) * 128:(i // 2) * 128 + 128]
                tr = S.emit("tensor", lambda e, slot=slot, h=h, n=n: e.transpose(
                    out=slot, in_=ktv[:, h, n * 128:n * 128 + 128],
                    identity=t_ident[:, :]),
                    waits=[tok_k, BG[tkey]], inc=("te", 1))
                trev = S.emit("scalar", lambda e, slot=slot: e.copy(
                    out=ktr[:, :], in_=slot),
                    waits=[tr, last_ktr_user], inc=("se", 1))
                BG[tkey] = trev
                amm.append(S.emit("tensor", lambda e, h=h, n=n, i=i, ba=ba: e.matmul(
                    out=P[ba][:, i * 128:(i + 1) * 128],
                    lhsT=ktv[:, h, n * 128:n * 128 + 128],
                    rhs=qtv[:, h, n * 128:n * 128 + 128], start=True, stop=True),
                    waits=[tok_q, BG[ba] if i == 0 else None], inc=("te", 1)))
                kv = S.emit("tensor", lambda e, h=h, n=n, i=i, bkv=bkv: e.matmul(
                    out=P[bkv][:, i * 128:(i + 1) * 128], lhsT=ktr[:, :],
                    rhs=vtv[:, n, h * 128:h * 128 + 128], start=True, stop=True),
                    waits=[trev, tok_v, BG[bkv] if i == 0 else None],
                    inc=("te", 1))
                kv_toks.append(kv)
                last_ktr_user = kv
            mb = t_mask[:, None, :].broadcast_to((128, 4, 128))
            am = S.emit("vector", lambda e, ba=ba, g=g: e.tensor_tensor(
                out=sc_am[g % 2][:, :].rearrange("p (a u) -> p a u", a=4),
                in0=P[ba][:, :].rearrange("p (a u) -> p a u", a=4),
                in1=mb, op=ALU.mult),
                waits=amm + [am_free[g % 2], tok_tab, tok_gcb], inc=("ve", 1))
            BG[ba] = am
            omm = []
            for i, h in enumerate(hs):
                def ofn(e, h=h, n=n, i=i, g=g, bo=bo):
                    ins = None
                    if n > 0:
                        ins = e.matmul(out=P[bo][:, i * 128:(i + 1) * 128],
                                       lhsT=sbBv[:, h],
                                       rhs=qtv[:, h, n * 128:n * 128 + 128],
                                       start=True, stop=False)
                    ins = e.matmul(out=P[bo][:, i * 128:(i + 1) * 128],
                                   lhsT=vtv[:, n, h * 128:h * 128 + 128],
                                   rhs=sc_am[g % 2][:, i * 128:(i + 1) * 128],
                                   start=(n == 0), stop=True)
                    return ins
                omm.append(S.emit("tensor", ofn,
                                  waits=[am, tok_sbb[g],
                                         BG[bo] if i == 0 else None],
                                  inc=("te", 1)))
            am_free[g % 2] = omm[-1]
            oev = S.emit("vector", lambda e, n=n, g=g, bo=bo: e.tensor_copy(
                out=oov[:, 4 * g:4 * g + 4, n * 128:n * 128 + 128],
                in_=P[bo][:, :].rearrange("p (a u) -> p a u", a=4)),
                waits=omm, inc=("ve", 1))
            BG[bo] = oev
            last_oev[(n, g)] = oev
            u1 = S.emit("vector", lambda e, g=g, bkv=bkv: e.tensor_add(
                sc_st[:, :], P[bkv][:, :],
                sbSv[:, 4 * g:4 * g + 4].rearrange("p a u -> p (a u)")),
                waits=kv_toks + [tok_state[g]], inc=("ve", 1))
            BG[bkv] = u1
            u2 = S.emit("vector", lambda e, g=g: e.tensor_mul(
                sbSv[:, 4 * g:4 * g + 4].rearrange("p a u -> p (a u)"),
                sc_st[:, :],
                gcbv[:, 4 * g:4 * g + 4].rearrange("p a u -> p (a u)")),
                waits=[u1], inc=("ve", 1))
            u3 = S.emit("vector", lambda e, g=g: e.tensor_mul(
                sbBv[:, 4 * g:4 * g + 4].rearrange("p a u -> p (a u)"),
                sc_st[:, :],
                gcbv[:, 4 * g:4 * g + 4].rearrange("p a u -> p (a u)")),
                waits=[u1], inc=("ve", 1))
            tok_state[g] = u2
            tok_sbb[g] = u3
    tok_scan_S = [tok_state[g] for g in range(4)]
    tok_scan_sbb = [tok_sbb[g] for g in range(4)]
    tok_scan_o = [last_oev[(NCH - 1, g)] for g in range(4)]
    tok_scan_done = tok_scan_S + tok_scan_sbb + tok_scan_o + \
        [am_free[0], am_free[1]]

    dumpable["oo_loc"] = (oo, tok_scan_o)
    dumpable["sbS"] = (sb_S, tok_scan_S)
    if PHASES < 4:
        tok_store = dma("sync", dout.rearrange("(a p) d -> p a d", p=128),
                        out_acc[:, :].rearrange("p (a d) -> p a d", a=NCH),
                        waits=tok_scan_done, sem="st")
        S.emit("sync", lambda e: e.nop(), waits=[tok_store])
        return _finish(nc, S, dumpable)
    w_l = dma("gpsimd", l_local.ap(), sb_S[:, :], waits=tok_scan_S, sem="ll")
    tok_cc = S.emit("gpsimd", lambda e: e.collective_compute(
        "AllGather", ALU.bypass, replica_groups=[list(range(NCORES))],
        ins=[l_local.ap().opt()], outs=[l_gather.ap().opt()]),
        waits=[w_l], inc=("cc", 1))

    tok_g, ws_free, _ = fm_proj(gatev, dW["Wg"], None, False, True,
                                ws_free, None)
    tok_gate = S.emit("vector", lambda e: e.tensor_scalar_mul(
        gate[:, :], gate[:, :], t_gnw[:, 0:1]),
        waits=[tok_g, tok_tab], inc=("ve", 1))

    lg = l_gather.ap()
    wtv = t_wtab[:, :].rearrange("p (c h) -> p c h", c=NCORES)
    stage_free = [None, None]
    tok_z = S.emit("vector", lambda e: e.memset(sb_S[:, :], 0.0),
                   waits=tok_scan_S + [w_l], inc=("ve", 1))
    tok_comb = tok_z
    for cp in range(NCORES):
        st = dma("sync", sc_stage[cp % 2][:, :], lg[cp * 128:(cp + 1) * 128, :],
                 waits=[tok_cc, stage_free[cp % 2]] +
                       (tok_scan_done if cp < 2 else []), sem="sg%d" % (cp % 2))
        for h in range(H):
            tok_comb = S.emit("vector", lambda e, cp=cp, h=h: e.scalar_tensor_tensor(
                out=sbSv[:, h], in0=sc_stage[cp % 2][:, :].rearrange(
                    "p (a u) -> p a u", a=H)[:, h],
                scalar=wtv[:, cp, h:h + 1], in1=sbSv[:, h],
                op0=ALU.mult, op1=ALU.add),
                waits=[st, tok_comb], inc=("ve", 1))
        stage_free[cp % 2] = tok_comb
    tok_sin_bf = S.emit("vector", lambda e: e.tensor_copy(
        out=sbb_S[:, :], in_=sb_S[:, :]),
        waits=[tok_comb] + tok_scan_sbb, inc=("ve", 1))

    dumpable["gate"] = (gate, [tok_gate])
    dumpable["sin"] = (sb_S, [tok_sin_bf])
    if PHASES < 5:
        tok_store = dma("sync", dout.rearrange("(a p) d -> p a d", p=128),
                        out_acc[:, :].rearrange("p (a d) -> p a d", a=NCH),
                        waits=[tok_sin_bf, tok_gate], sem="st")
        S.emit("sync", lambda e: e.nop(), waits=[tok_store])
        return _finish(nc, S, dumpable)
    gamma_np = 1.0 - np.exp2(-5.0 - np.arange(H, dtype=np.float64))
    gC_np = gamma_np ** C
    tok_corr = []
    for n in range(NCH):
        for g in range(4):
            ba = g % 2
            cmm = []
            for i in range(4):
                h = 4 * g + i
                cmm.append(S.emit("tensor", lambda e, h=h, n=n, i=i, ba=ba: e.matmul(
                    out=P[ba][:, i * 128:(i + 1) * 128], lhsT=sbBv[:, h],
                    rhs=qtv[:, h, n * 128:n * 128 + 128], start=True, stop=True),
                    waits=[tok_sin_bf, tok_g if n == 0 and g < 2 else None,
                           BG[ba] if i == 0 else None], inc=("te", 1)))
            last = None
            for i in range(4):
                h = 4 * g + i
                last = S.emit("vector",
                    lambda e, h=h, n=n, i=i, ba=ba: e.scalar_tensor_tensor(
                        out=oov[:, h, n * 128:n * 128 + 128],
                        in0=P[ba][:, i * 128:(i + 1) * 128],
                        scalar=float(gC_np[h] ** n),
                        in1=oov[:, h, n * 128:n * 128 + 128],
                        op0=ALU.mult, op1=ALU.add),
                    waits=[cmm[3]], inc=("ve", 1))
            BG[ba] = last
            if n == NCH - 1:
                tok_corr.append(last)

    tok_og = None
    sq_free = None
    row_free = {}
    for h in range(H):
        sq = S.emit("vector", lambda e, h=h: e.tensor_mul(
            sc_osq[:, :], oov[:, h, :], oov[:, h, :]),
            waits=tok_corr + [sq_free], inc=("ve", 1))
        for t2 in range(2):
            bs = 4 + h % 2
            smm = S.emit("tensor", lambda e, t2=t2, bs=bs: e.matmul(
                out=P[bs][0:1, :], lhsT=t_onescol[:, :],
                rhs=sc_osq[:, t2 * 512:t2 * 512 + 512], start=True, stop=True),
                waits=[sq, BG[bs], tok_tab], inc=("te", 1))
            sq_free = smm
            sqrow = S.emit("scalar", lambda e, t2=t2, bs=bs: e.activation(
                out=sc_row[0:1, t2 * 512:t2 * 512 + 512], in_=P[bs][0:1, :],
                func=AF.Sqrt, scale=1.0 / 128.0, bias=EPS),
                waits=[smm, row_free.get((h - 1, t2))], inc=("se", 1))
            BG[bs] = sqrow
            rs = S.emit("vector", lambda e, t2=t2: e.reciprocal(
                sc_row2[0:1, t2 * 512:t2 * 512 + 512],
                sc_row[0:1, t2 * 512:t2 * 512 + 512]), waits=[sqrow],
                inc=("ve", 1))
            row_free[(h, t2)] = rs
            bb = h % 2
            bmm = S.emit("tensor", lambda e, t2=t2, bb=bb: e.matmul(
                out=P[bb][:, :], lhsT=t_onesrow[:, :],
                rhs=sc_row2[0:1, t2 * 512:t2 * 512 + 512], start=True, stop=True),
                waits=[rs, BG[bb]], inc=("te", 1))
            t1 = S.emit("vector", lambda e, t2=t2, h=h, bb=bb: e.tensor_mul(
                sc_tmp[:, :], P[bb][:, :], oov[:, h, t2 * 512:t2 * 512 + 512]),
                waits=[bmm], inc=("ve", 1))
            BG[bb] = t1
            tok_og = S.emit("vector", lambda e, t2=t2, h=h: e.tensor_mul(
                ogv[:, h, t2 * 512:t2 * 512 + 512], sc_tmp[:, :],
                gatev[:, h, t2 * 512:t2 * 512 + 512]),
                waits=[t1, tok_gate], inc=("ve", 1))

    dumpable["oo"] = (oo, tok_corr)
    dumpable["og"] = (og, [tok_og])
    if PHASES < 6:
        tok_store = dma("sync", dout.rearrange("(a p) d -> p a d", p=128),
                        out_acc[:, :].rearrange("p (a d) -> p a d", a=NCH),
                        waits=[tok_og], sem="st")
        S.emit("sync", lambda e: e.nop(), waits=[tok_store])
        return _finish(nc, S, dumpable)
    tok_b2a = dma("gpsimd", sbb_b2[0:NE, :], drb2, waits=tok_corr, sem="ms")
    tok_b2b = dma("gpsimd", sbb_b2[NE:18, :], dsb2, waits=[tok_b2a], sem="ms")
    tok_wa = dma("gpsimd", sb_wa[:, :].rearrange("p (c eN) -> p c eN", c=DC),
                 dwa.rearrange("(c p) eN -> p c eN", p=128),
                 waits=[tok_b2b], sem="ms")
    wav = sb_wa[:, :].rearrange("p (c eN) -> p c eN", c=DC)
    tok_comb_tt = []
    for tt in range(NCH):
        bank = 2 + tt % 2
        def affmm(e, tt=tt, bank=bank):
            ins = None
            for dc in range(DC):
                ins = e.matmul(out=P[bank][:, 0:NE],
                               lhsT=xnv[:, dc, tt * 128:tt * 128 + 128],
                               rhs=wav[:, dc], start=(dc == 0),
                               stop=(dc == DC - 1))
            return ins
        mm = S.emit("tensor", affmm, waits=[tok_wa, BG[bank]], inc=("te", 1))
        sg_ = S.emit("scalar", lambda e, bank=bank: e.activation(
            out=sc_aff[:, :], in_=P[bank][:, 0:NE], func=AF.Sigmoid),
            waits=[mm, tok_og if tt == 0 else None,
                   tok_comb_tt[-1] if tt else None], inc=("se", 1))
        BG[bank] = sg_
        m1 = S.emit("vector", lambda e: e.reduce_max(
            out=t_stat[:, 0:1], in_=sc_aff[:, :], axis=AXX), waits=[sg_],
            inc=("ve", 1))
        mk1 = S.emit("vector", lambda e: e.tensor_single_scalar(
            out=sc_am1[:, :], in_=sc_aff[:, :], scalar=t_stat[:, 0:1],
            op=ALU.is_ge), waits=[m1], inc=("ve", 1))
        t1 = S.emit("vector", lambda e: e.tensor_scalar_mul(
            sc_am2[:, :], sc_am1[:, :], 1e30), waits=[mk1], inc=("ve", 1))
        a2 = S.emit("vector", lambda e: e.tensor_sub(
            sc_am1[:, :], sc_aff[:, :], sc_am2[:, :]), waits=[t1], inc=("ve", 1))
        m2 = S.emit("vector", lambda e: e.reduce_max(
            out=t_stat[:, 1:2], in_=sc_am1[:, :], axis=AXX), waits=[a2],
            inc=("ve", 1))
        mk2 = S.emit("vector", lambda e: e.tensor_single_scalar(
            out=sc_am2[:, :], in_=sc_aff[:, :], scalar=t_stat[:, 1:2],
            op=ALU.is_ge), waits=[m2], inc=("ve", 1))
        dn = S.emit("vector", lambda e: e.tensor_add(
            t_stat[:, 2:3], t_stat[:, 0:1], t_stat[:, 1:2]), waits=[m2],
            inc=("ve", 1))
        rc = S.emit("vector", lambda e: e.reciprocal(
            t_stat[:, 3:4], t_stat[:, 2:3]), waits=[dn], inc=("ve", 1))
        t2_ = S.emit("vector", lambda e: e.tensor_mul(
            sc_am1[:, :], sc_am2[:, :], sc_aff[:, :]), waits=[mk2],
            inc=("ve", 1))
        tok_comb_tt.append(S.emit("vector", lambda e, tt=tt: e.tensor_scalar_mul(
            combv[:, tt], sc_am1[:, :], t_stat[:, 3:4]), waits=[t2_, rc],
            inc=("ve", 1)))
    tok_combine = tok_comb_tt[-1]
    tok_ct0 = S.emit("vector", lambda e: e.memset(sb_cT[:, :], 1.0),
                     waits=tok_corr, inc=("ve", 1))
    tok_cbf = S.emit("vector", lambda e: e.tensor_copy(
        out=t_combbf[:, :], in_=t_comb[:, :]), waits=[tok_combine],
        inc=("ve", 1))
    cbv = t_combbf[:, :].rearrange("p (a eN) -> p a eN", a=NCH)
    tok_ct = None
    for tt in range(NCH):
        tkey = "T%d" % (tt % 2)
        tr = S.emit("tensor", lambda e, tt=tt: e.transpose(
            out=PT[tt % 2][0:NE, 0:128], in_=cbv[:, tt],
            identity=t_ident[:, :]), waits=[tok_cbf, BG[tkey]], inc=("te", 1))
        tok_ct = S.emit("scalar", lambda e, tt=tt: e.copy(
            out=sb_cT[0:NE, tt * 128:tt * 128 + 128],
            in_=PT[tt % 2][0:NE, 0:128]),
            waits=[tr, tok_ct0 if tt == 0 else None], inc=("se", 1))
        BG[tkey] = tok_ct

    dumpable["comb"] = (t_comb, [tok_combine])
    dumpable["cT"] = (sb_cT, [tok_ct])
    dumpable["b2"] = (sbb_b2, [tok_b2b])
    if PHASES < 7:
        tok_store = dma("sync", dout.rearrange("(a p) d -> p a d", p=128),
                        out_acc[:, :].rearrange("p (a d) -> p a d", a=NCH),
                        waits=[tok_combine, tok_ct, tok_b2b, tok_idE], sem="st")
        S.emit("sync", lambda e: e.nop(), waits=[tok_store])
        return _finish(nc, S, dumpable)
    tok_idE = dma("sync", sb_idE[:, :], dtab["t_identE"],
                  waits=tok_corr + [tok_ct], sem="ie")

    wor = dW["Wo"].rearrange("(c p) f -> p c f", p=128)
    wov = [b[:, :].rearrange("p (c f) -> p c f", c=DC) for b in wobuf]
    idEv = sb_idE[:, :].rearrange("p (a d) -> p a d", a=4)
    wo_free = [tok_og, tok_og]
    tok_oa_last = None
    for dt in range(4):
        wdma = dma("gpsimd", wobuf[dt % 2][:, :],
                   wor[:, :, dt * 512:(dt + 1) * 512],
                   waits=[wo_free[dt % 2]], sem="o%d" % (dt % 2))
        for tt in range(NCH):
            bank = tt % 2
            def oamm(e, dt=dt, tt=tt, bank=bank):
                ins = None
                for hc in range(H):
                    ins = e.matmul(out=P[bank][:, :],
                                   lhsT=ogv[:, hc, tt * 128:tt * 128 + 128],
                                   rhs=wov[dt % 2][:, hc],
                                   start=(hc == 0), stop=False)
                for il in range(4):
                    dc = dt * 4 + il
                    ins = e.matmul(out=P[bank][:, :],
                                   lhsT=xnv[:, dc, tt * 128:tt * 128 + 128],
                                   rhs=idEv[:, il], start=False, stop=False)
                ins = e.matmul(out=P[bank][:, :],
                               lhsT=sb_cT[:, tt * 128:tt * 128 + 128],
                               rhs=sbb_b2[:, dt * 512:(dt + 1) * 512],
                               start=False, stop=True)
                return ins
            mm = S.emit("tensor", oamm,
                        waits=[wdma, tok_og, tok_ct, tok_b2b, tok_idE,
                               BG[bank]], inc=("te", 1))
            if tt == NCH - 1:
                wo_free[dt % 2] = mm
            tok_oa_last = S.emit("scalar", lambda e, dt=dt, tt=tt, bank=bank: e.copy(
                out=oav[:, tt, dt * 512:(dt + 1) * 512], in_=P[bank][:, :]),
                waits=[mm], inc=("se", 1))
            BG[bank] = tok_oa_last

    dumpable["oinit"] = (out_acc, [tok_oa_last])
    if PHASES < 8:
        tok_store = dma("sync", dout.rearrange("(a p) d -> p a d", p=128),
                        out_acc[:, :].rearrange("p (a d) -> p a d", a=NCH),
                        waits=[tok_oa_last], sem="st")
        S.emit("sync", lambda e: e.nop(), waits=[tok_store])
        return _finish(nc, S, dumpable)
    rgr = drg.rearrange("eN (c p) f -> eN p c f", p=128)
    rw1r = drw1.rearrange("eN (c p) f -> eN p c f", p=128)
    rw2r = drw2.rearrange("eN (c p) d -> eN p c d", p=128)
    sgr = dsg.rearrange("eN (c p) f -> eN p c f", p=128)
    sw1r = dsw1.rearrange("eN (c p) f -> eN p c f", p=128)
    sw2r = dsw2.rearrange("eN (c p) d -> eN p c d", p=128)
    w2v = [b[:, :].rearrange("p (c d) -> p c d", c=FC) for b in w2buf]
    yv = ybuf[:, :].rearrange("p (c t) -> p c t", c=FC)
    ws_free = [tok_oa_last] * 4
    w2_free = [tok_oa_last] * 3
    y_free = [tok_oa_last] * FC
    gel_free = [None, None]
    tok_oadd = {dt: tok_oa_last for dt in range(4)}
    w2slot = 0
    for e_i in range(18):
        routed = e_i < NE
        if routed:
            gsrc, w1src, w2src = rgr[e_i], rw1r[e_i], rw2r[e_i]
            gb = lambda fc, e_i=e_i: t_bias[:, e_i * 8 + fc:e_i * 8 + fc + 1]
            b1 = lambda fc, e_i=e_i: t_bias[:, 128 + e_i * 8 + fc:
                                            128 + e_i * 8 + fc + 1]
        else:
            sh = e_i - NE
            gsrc, w1src, w2src = sgr[sh], sw1r[sh], sw2r[sh]
            gb = lambda fc, sh=sh: t_bias[:, 256 + sh * 8 + fc:
                                          256 + sh * 8 + fc + 1]
            b1 = lambda fc, sh=sh: t_bias[:, 272 + sh * 8 + fc:
                                          272 + sh * 8 + fc + 1]
        tok_y = {}
        for fc in range(FC):
            sgs, s1s = (2 * fc) % 4, (2 * fc + 1) % 4
            wg_ = dma("gpsimd", wsbuf[sgs][:, :], gsrc[:, :, fc * 128:(fc + 1) * 128],
                      waits=[ws_free[sgs]], sem="w%d" % sgs)
            w1_ = dma("gpsimd", wsbuf[s1s][:, :], w1src[:, :, fc * 128:(fc + 1) * 128],
                      waits=[ws_free[s1s]], sem="w%d" % s1s)
            for t2 in range(2):
                bg_, b1_ = (2 * fc + 2 * t2) % 4, (2 * fc + 2 * t2 + 1) % 4
                def hmm(e, fc=fc, t2=t2, bg_=bg_, b1_=b1_, sgs=sgs, s1s=s1s):
                    ins = None
                    for dc in range(DC):
                        e.matmul(out=P[bg_][:, :], lhsT=wsv[sgs][:, dc],
                                 rhs=xnv[:, dc, t2 * 512:t2 * 512 + 512],
                                 start=(dc == 0), stop=(dc == DC - 1))
                    for dc in range(DC):
                        ins = e.matmul(out=P[b1_][:, :], lhsT=wsv[s1s][:, dc],
                                       rhs=xnv[:, dc, t2 * 512:t2 * 512 + 512],
                                       start=(dc == 0), stop=(dc == DC - 1))
                    return ins
                mm = S.emit("tensor", hmm,
                            waits=[wg_, w1_, y_free[fc], BG[bg_], BG[b1_]],
                            inc=("te", 1))
                if t2 == 1:
                    ws_free[sgs] = mm
                    ws_free[s1s] = mm
                ge_ = S.emit("scalar", lambda e, fc=fc, t2=t2, bg_=bg_: e.activation(
                    out=sc_gel[t2][:, :], in_=P[bg_][:, :], func=AF.Gelu,
                    bias=gb(fc)), waits=[mm, tok_tab, gel_free[t2]],
                    inc=("se", 1))
                BG[bg_] = ge_
                ytok = S.emit("vector",
                    lambda e, fc=fc, t2=t2, b1_=b1_: e.scalar_tensor_tensor(
                        out=yv[:, fc, t2 * 512:t2 * 512 + 512], in0=P[b1_][:, :],
                        scalar=b1(fc), in1=sc_gel[t2][:, :],
                        op0=ALU.add, op1=ALU.mult), waits=[ge_], inc=("ve", 1))
                BG[b1_] = ytok
                gel_free[t2] = ytok
                tok_y[(fc, t2)] = ytok
        last_w2mm = None
        for dh in range(2):
            slot = w2slot
            w2slot = (w2slot + 1) % 3
            wdma = dma("gpsimd", w2buf[slot][:, :],
                       w2src[:, :, dh * 1024:(dh + 1) * 1024],
                       waits=[w2_free[slot]], sem="u%d" % slot)
            for tt in range(NCH):
                mmw = [tok_y[(fc, tt // 4)] for fc in range(FC)]
                for dt2 in range(2):
                    dt = dh * 2 + dt2
                    bank = 4 + (tt + dt2) % 2
                    def w2mm(e, slot=slot, tt=tt, dt2=dt2, bank=bank):
                        ins = None
                        for fc in range(FC):
                            ins = e.matmul(
                                out=P[bank][:, :],
                                lhsT=yv[:, fc, tt * 128:tt * 128 + 128],
                                rhs=w2v[slot][:, fc, dt2 * 512:dt2 * 512 + 512],
                                start=(fc == 0), stop=(fc == FC - 1))
                        return ins
                    mm = S.emit("tensor", w2mm,
                                waits=[wdma] + mmw + [BG[bank]], inc=("te", 1))
                    last_w2mm = mm
                    if tt == NCH - 1 and dt2 == 1:
                        w2_free[slot] = mm
                    if routed:
                        ad = S.emit("vector",
                            lambda e, tt=tt, dt=dt, e_i=e_i, bank=bank:
                            e.scalar_tensor_tensor(
                                out=oav[:, tt, dt * 512:(dt + 1) * 512],
                                in0=P[bank][:, :],
                                scalar=combv[:, tt, e_i:e_i + 1],
                                in1=oav[:, tt, dt * 512:(dt + 1) * 512],
                                op0=ALU.mult, op1=ALU.add),
                            waits=[mm, tok_combine], inc=("ve", 1))
                    else:
                        ad = S.emit("vector", lambda e, tt=tt, dt=dt, bank=bank:
                            e.tensor_add(
                                oav[:, tt, dt * 512:(dt + 1) * 512],
                                P[bank][:, :],
                                oav[:, tt, dt * 512:(dt + 1) * 512]),
                            waits=[mm], inc=("ve", 1))
                    BG[bank] = ad
                    if tt == NCH - 1:
                        tok_oadd[dt] = ad
        for fc in range(FC):
            y_free[fc] = last_w2mm
    tok_ffn = [tok_oadd[dt] for dt in range(4)]

    tok_xacc = dma("gpsimd", out_acc[:, :].rearrange("p (a d) -> p a d", a=NCH),
                   xr, waits=tok_ffn, sem="xa", accum=ALU.add)
    outr = dout.rearrange("(a p) d -> p a d", p=128)
    tok_store = dma("sync", outr,
                    out_acc[:, :].rearrange("p (a d) -> p a d", a=NCH),
                    waits=[tok_xacc], sem="st")
    S.emit("sync", lambda e: e.nop(), waits=[tok_store])

    return _finish(nc, S, dumpable)


def _finish(nc, S, dumpable=None):
    if dumpable:
        for nm in DUMP:
            if nm not in dumpable:
                continue
            t, wt = dumpable[nm]
            dd = nc.dram_tensor("dbg_" + nm, [t.shape[0], t.shape[1]], t.dtype,
                                kind="ExternalOutput").ap()
            tok = S.emit("sync", lambda e, dd=dd, t=t: e.dma_start(
                out=dd, in_=t[:, :]), waits=wt, inc=("st", 16))
            S.emit("sync", lambda e: e.nop(), waits=[tok])
    from contextlib import ExitStack
    ctx = ExitStack()
    used = set(S.cnt.keys())
    for ops in S.ops.values():
        for waits, _, incs in ops:
            used.update(x[0] for x in waits)
            used.update(x[0] for x in incs)
    sems = {s_: ctx.enter_context(nc.semaphore(s_)) for s_ in sorted(used)}
    with nc.Block() as block:
        block.sync(lambda e: S.replay("sync", e, sems))
        block.gpsimd(lambda e: S.replay("gpsimd", e, sems))
        block.tensor(lambda e: S.replay("tensor", e, sems))
        block.vector(lambda e: S.replay("vector", e, sems))
        block.scalar(lambda e: S.replay("scalar", e, sems))
    ctx.close()
    return nc


def kernel(**inputs):
    x = np.ascontiguousarray(np.asarray(inputs["L0_attn_x"], np.float32)[0])
    biasc = np.zeros((128, 288), np.float32)
    rgb = np.asarray(inputs["L0_ffn_rgb"], np.float32)
    rb1 = np.asarray(inputs["L0_ffn_rb1"], np.float32)
    sgb = np.asarray(inputs["L0_ffn_sgb"], np.float32)
    sb1 = np.asarray(inputs["L0_ffn_sb1"], np.float32)
    biasc[:, 0:128] = rgb.reshape(NE, FC, 128).transpose(2, 0, 1).reshape(128, 128)
    biasc[:, 128:256] = rb1.reshape(NE, FC, 128).transpose(2, 0, 1).reshape(128, 128)
    biasc[:, 256:272] = sgb.reshape(NSH, FC, 128).transpose(2, 0, 1).reshape(128, 16)
    biasc[:, 272:288] = sb1.reshape(NSH, FC, 128).transpose(2, 0, 1).reshape(128, 16)
    common = {
        "Wq": np.ascontiguousarray(inputs["L0_attn_Wq"], np.float32),
        "Wk": np.ascontiguousarray(inputs["L0_attn_Wk"], np.float32),
        "Wv": np.ascontiguousarray(inputs["L0_attn_Wv"], np.float32),
        "Wg": np.ascontiguousarray(inputs["L0_attn_Wg"], np.float32),
        "Wo": np.ascontiguousarray(inputs["L0_attn_Wo"], np.float32),
        "wa": np.ascontiguousarray(inputs["L0_ffn_wa"], np.float32),
        "gnw": np.ascontiguousarray(
            np.asarray(inputs["L0_attn_g_norm_w"], np.float32).reshape(128, 1)),
        "rg": np.ascontiguousarray(inputs["L0_ffn_rg"], np.float32),
        "rw1": np.ascontiguousarray(inputs["L0_ffn_rw1"], np.float32),
        "rw2": np.ascontiguousarray(inputs["L0_ffn_rw2"], np.float32),
        "sg": np.ascontiguousarray(inputs["L0_ffn_sg"], np.float32),
        "sw1": np.ascontiguousarray(inputs["L0_ffn_sw1"], np.float32),
        "sw2": np.ascontiguousarray(inputs["L0_ffn_sw2"], np.float32),
        "biasc": biasc,
        "rb2": np.ascontiguousarray(inputs["L0_ffn_rb2"], np.float32),
        "sb2": np.ascontiguousarray(inputs["L0_ffn_sb2"], np.float32),
    }
    nc = build_program()
    in_maps = []
    for c in range(NCORES):
        m = dict(common)
        m["xs"] = np.ascontiguousarray(x[c * TPC:(c + 1) * TPC])
        m.update(host_tables(c))
        in_maps.append(m)
    res = run_bass_kernel_spmd(nc, in_maps, core_ids=list(range(NCORES)))
    out = np.concatenate([r["out"] for r in res.results], axis=0)[None]
    return np.ascontiguousarray(out.astype(np.float32))
